# version 08
# baseline (speedup 1.0000x reference)
import numpy as np
from contextlib import ExitStack

import concourse.bass as bass
import concourse.mybir as mybir
from concourse.bass_utils import run_bass_kernel_spmd

F32 = mybir.dt.float32
BF16 = mybir.dt.bfloat16
U8 = mybir.dt.uint8
AF = mybir.ActivationFunctionType
ALU = mybir.AluOpType
AX = mybir.AxisListType

NCORE = 8
D = 2048
HC = 1024
HEADS = 16
DH = 64
T = 512
NT = 2048
NW = NT // T
KD = D // 128
KH = HC // 128
NB = NT // 512
INTER = 8192
CI = INTER // 128
EPS = 1e-5
SCALE = 0.125
INVD = 1.0 / float(D)

ENGS = ("sync", "gpsimd", "tensor", "vector", "scalar")

DEBUG_TAPS = []
PARANOID = set()


class Prog:

    def __init__(self):
        self.ops = {e: [] for e in ENGS}
        self.cnt = {}

    def sem(self, *names):
        for n in names:
            self.cnt[n] = 0

    def op(self, eng, fn, inc=None, n=1):
        if inc is None:
            self.ops[eng].append(("o", fn, None, 0))
            return None
        self.cnt[inc] += n
        self.ops[eng].append(("i", fn, inc, n))
        return self.cnt[inc]

    def wait(self, eng, sem, val):
        if val is None or val <= 0:
            return
        assert val <= self.cnt[sem], (eng, sem, val, self.cnt[sem])
        self.ops[eng].append(("w", None, sem, val))

    def dma(self, eng, out, in_, sem):
        return self.op(eng, (lambda e, o=out, i=in_: e.dma_start(o, i)), inc=sem, n=16)

    def barrier(self, name):
        if not ({"all", name} & PARANOID):
            return
        vals = {}
        for e in ENGS:
            vals[e] = self.op(e, (lambda h: h.nop()), inc="b_" + e)
        for e in ENGS:
            for o in ENGS:
                if o != e:
                    self.wait(e, "b_" + o, vals[o])

    def replay(self, eng_name, handle, sems):
        for kind, fn, sem, n in self.ops[eng_name]:
            if kind == "w":
                handle.wait_ge(sems[sem], n)
            elif kind == "i":
                fn(handle).then_inc(sems[sem], n)
            else:
                fn(handle)


def build():
    nc = bass.Bass("TRN2", target_bir_lowering=False)

    xT = nc.dram_tensor("xT", [D, NT], F32, kind="ExternalInput").ap()
    wc = nc.dram_tensor("wc", [D, HC], F32, kind="ExternalInput").ap()
    wq = nc.dram_tensor("wq", [HC, HC], F32, kind="ExternalInput").ap()
    wk = nc.dram_tensor("wk", [HC, HC], F32, kind="ExternalInput").ap()
    wv = nc.dram_tensor("wv", [HC, HC], F32, kind="ExternalInput").ap()
    wd = nc.dram_tensor("wd", [HC, D], F32, kind="ExternalInput").ap()
    wgu = nc.dram_tensor("wgu", [D, 2 * INTER], F32, kind="ExternalInput").ap()
    wdn = nc.dram_tensor("wdn", [INTER, D], F32, kind="ExternalInput").ap()
    bc_in = nc.dram_tensor("bc", [128, KH], F32, kind="ExternalInput").ap()
    bq_in = nc.dram_tensor("bq", [128, KH], F32, kind="ExternalInput").ap()
    bk_in = nc.dram_tensor("bk", [128, KH], F32, kind="ExternalInput").ap()
    bd_in = nc.dram_tensor("bd", [128, KD], F32, kind="ExternalInput").ap()
    bvr_in = nc.dram_tensor("bvr", [1, HC], F32, kind="ExternalInput").ap()
    bmat_in = nc.dram_tensor("bmat", [T, T], F32, kind="ExternalInput").ap()
    ident_in = nc.dram_tensor("ident", [128, 128], F32, kind="ExternalInput").ap()

    outT = nc.dram_tensor("outT", [D, NT], F32, kind="ExternalOutput").ap()
    rscr = nc.dram_tensor("rscr", [D, NT], F32, kind="Internal").ap()
    h1scr = nc.dram_tensor("h1scr", [D, NT], F32, kind="Internal").ap()
    r2scr = nc.dram_tensor("r2scr", [D, NT], F32, kind="Internal").ap()

    taps = {}
    for name in DEBUG_TAPS:
        shp = {"xc": [HC, NT], "q": [HC, NT], "k": [HC, NT], "v": [NT, HC],
               "ctx": [HC, NT], "r": [D, NT], "h1": [D, NT], "act": [INTER, T],
               "r2": [D, NT]}[name]
        taps[name] = nc.dram_tensor("tap_" + name, shp, F32, kind="ExternalOutput").ap()

    p = Prog()
    p.sem("dg", "dsl", "dst", "pe", "ac", "ve", "gp")
    p.sem(*["b_" + e for e in ENGS])

    with ExitStack() as ctx:
        E = ctx.enter_context

        wslot = [E(nc.sbuf_tensor(f"wsl{i}", [128, 2048], BF16)) for i in range(4)]
        uslot = [E(nc.sbuf_tensor(f"usl{i}", [128, 2048], BF16)) for i in range(4)]
        dnslot = [E(nc.sbuf_tensor(f"dnsl{i}", [128, 1024], BF16)) for i in range(4)]
        padg_buf = E(nc.sbuf_tensor("padg", [128, 512], BF16))
        pads_buf = E(nc.sbuf_tensor("pads", [128, 256], F32))
        identb = E(nc.sbuf_tensor("identb", [128, 128], BF16))
        ones_colb = E(nc.sbuf_tensor("ones_colb", [128, 1], BF16))
        ones_rowb = E(nc.sbuf_tensor("ones_rowb", [1, 128], BF16))
        ones_row32 = E(nc.sbuf_tensor("ones_row32", [1, 128], F32))
        bc_s = E(nc.sbuf_tensor("bc_s", [128, KH], F32))
        bq_s = E(nc.sbuf_tensor("bq_s", [128, KH], F32))
        bk_s = E(nc.sbuf_tensor("bk_s", [128, KH], F32))
        bd_s = E(nc.sbuf_tensor("bd_s", [128, KD], F32))
        mst = E(nc.sbuf_tensor("mst", [128, 8], F32))
        lst = E(nc.sbuf_tensor("lst", [128, 8], F32))
        rlst = E(nc.sbuf_tensor("rlst", [128, 8], F32))

        AR = 160 * 1024
        arena = E(nc.sbuf_tensor("arena", [128, AR], U8))
        ps = E(nc.psum_tensor("ps", [128, 4096], F32))

        semh = {}
        for s in list(p.cnt):
            semh[s] = E(nc.semaphore(s))

        def carve(off, cols, dtype, rows=128, row0=0):
            bs = 4 if dtype is F32 else 2
            return arena[row0:row0 + rows, off:off + cols * bs].bitcast(dtype)

        KB = 1024
        xbf = lambda kc: carve(kc * 4 * KB, NT, BF16)
        xcv = lambda c: carve(64 * KB + c * 4 * KB, NT, BF16)
        qv = lambda c: carve(c * 4 * KB, NT, BF16)
        kv = lambda c: carve(32 * KB + c * 4 * KB, NT, BF16)
        vv = lambda tm: carve(96 * KB + tm * 2 * KB, HC, BF16)
        wvs = lambda k8, n2: carve(128 * KB + (2 * k8 + n2) * KB, 512, BF16)
        ctxv = lambda c: carve(128 * KB + c * 4 * KB, NT, BF16)
        Spv = lambda i: carve(64 * KB + i * 2 * KB, 512, F32)
        Ptv = lambda i: carve(68 * KB + i * KB, 512, BF16)
        PTv = lambda s: carve(76 * KB + s * 4 * KB, NT, BF16)
        xstv = lambda i: carve(i * 8 * KB, NT, F32)
        ystv = lambda i: carve(16 * KB + i * 2 * KB, 512, F32)
        siluv = lambda i: carve(20 * KB + i * 2 * KB, 512, F32)
        rstv = lambda i: carve(32 * KB + i * 8 * KB, NT, F32)
        rtbv = lambda i: carve(48 * KB + i * 4 * KB, NT, BF16)
        sqbv = lambda i: carve(56 * KB + i * 4 * KB, NT, BF16)
        mean_blk = carve(24 * KB, 512, F32, rows=1)
        rstd_blk = carve(26 * KB, 512, F32, rows=1)
        wrow = carve(28 * KB, 512, F32, rows=1)
        h1bv = lambda m: carve(64 * KB + m * KB, 512, BF16)
        actv = lambda j: carve(80 * KB + j * KB, 512, BF16)
        Bb = carve(84 * KB, 4 * 512, BF16)
        mbc_s = lambda b: carve(144 * KB + b * 2 * KB, 512, F32)
        rbc_s = lambda b: carve(152 * KB + b * 2 * KB, 512, F32)
        bv_b = carve(144 * KB, HC, BF16, rows=1)

        def psb(b, rows=128):
            return ps[0:rows, 512 * b:512 * b + 512]

        def psb_bf(b, half):
            bb = 2 if half == 0 else 5
            return ps[:, 512 * bb:512 * bb + 256].bitcast(BF16)

        def psrow(b):
            return ps[0:1, 512 * b:512 * b + 512]

        bank_war = {b: None for b in range(8)}

        def bank_wait(eng, b):
            if bank_war[b] is not None:
                p.wait(eng, bank_war[b][0], bank_war[b][1])

        war = {}

        def slot_wait(eng, key):
            if key in war and war[key] is not None:
                p.wait(eng, war[key][0], war[key][1])

        MM = lambda out, l, r, st, sp: (
            lambda e, o=out, a=l, b=r, s1=st, s2=sp: e.matmul(o, a, b, start=s1, stop=s2))

        def tap_sb(name, view_fn, nch, nfree, ready_sem, ready_val):
            if name not in taps:
                return
            p.wait("gpsimd", ready_sem, ready_val)
            for ch in range(nch):
                p.dma("gpsimd", taps[name][128 * ch:128 * ch + 128, 0:nfree],
                      view_fn(ch)[:, 0:nfree], "dg")

        def tap_dram(name, src_ap, ready_val):
            if name not in taps:
                return
            p.wait("sync", "dst", ready_val)
            p.dma("sync", taps[name], src_ap, "dst")

        def pad_g():
            return p.dma("gpsimd", padg_buf[:, :], wdn[0:128, 0:512], "dg")

        def pad_s():
            return p.dma("sync", pads_buf[:, :], xT[0:128, 0:256], "dsl")

        p.op("vector", lambda e: e.memset(ones_colb[:, :], 1.0))
        p.op("vector", lambda e: e.memset(ones_rowb[:, :], 1.0))
        v_ones = p.op("vector", lambda e: e.memset(ones_row32[:, :], 1.0), inc="ve")
        p.dma("gpsimd", identb[:, :], ident_in, "dg")
        p.dma("gpsimd", bv_b[:, :], bvr_in, "dg")
        for k8 in range(KH):
            for n2 in range(2):
                wv_ready = p.dma("gpsimd", wvs(k8, n2),
                                 wv[128 * k8:128 * k8 + 128, 512 * n2:512 * n2 + 512], "dg")
        for kc in range(KD):
            x_ready = p.dma("gpsimd", xbf(kc)[:, :], xT[128 * kc:128 * kc + 128, :], "dg")
        p.dma("sync", bc_s[:, :], bc_in, "dsl")
        p.dma("sync", bq_s[:, :], bq_in, "dsl")
        p.dma("sync", bk_s[:, :], bk_in, "dsl")
        bias_ready = p.dma("sync", bd_s[:, :], bd_in, "dsl")

        def proj(w_dram, nK, rhs_fn, out_fn, bias_t, rhs_ready_dg, rhs_ready_ac=None,
                 evac_scale=1.0):
            nM = w_dram.shape[1] // 128
            wre = w_dram.rearrange("(kc p) o -> p kc o", p=128)
            if rhs_ready_ac:
                p.wait("tensor", "ac", rhs_ready_ac)
            evac_of_group = {}
            ready = {}

            def issue(m):
                sl = wslot[m % 4]
                view = sl[:, 0:nK * 128].rearrange("p (kc o) -> p kc o", kc=nK)
                slot_wait("gpsimd", ("projw", m % 4))
                half = nK // 2
                p.dma("gpsimd", view[:, 0:half, :],
                      wre[:, 0:half, 128 * m:128 * m + 128], "dg")
                ready[m] = p.dma("gpsimd", view[:, half:nK, :],
                                 wre[:, half:nK, 128 * m:128 * m + 128], "dg")

            issue(0)
            issue(1)
            for m in range(nM):
                if m + 2 < nM:
                    issue(m + 2)
                if m + 1 not in ready:
                    ready[m + 1] = pad_g()
                p.wait("tensor", "dg", max(ready[m + 1], rhs_ready_dg))
                for n in range(NB):
                    g = m * NB + n
                    bank = g % 2
                    bank_wait("tensor", bank)
                    for kc in range(nK):
                        val = p.op("tensor",
                                   MM(psb(bank), wslot[m % 4][:, 128 * kc:128 * kc + 128],
                                      rhs_fn(kc)[:, 512 * n:512 * n + 512],
                                      kc == 0, kc == nK - 1),
                                   inc="pe" if kc == nK - 1 else None)
                    p.wait("scalar", "pe", val)
                    p.wait("scalar", "dsl", bias_ready)
                    ev = p.op("scalar",
                              (lambda e, o=out_fn(m), nn=n, bk=bank, mm=m:
                               e.activation(o[:, 512 * nn:512 * nn + 512], psb(bk),
                                            AF.Identity, bias=bias_t[:, mm:mm + 1],
                                            scale=evac_scale)),
                              inc="ac")
                    evac_of_group[g] = ev
                    bank_war[bank] = ("ac", ev)
                war[("projw", m % 4)] = ("ac", evac_of_group[m * NB + NB - 1])
            return evac_of_group[nM * NB - 1]

        p1_done = proj(wc, KD, xbf, xcv, bc_s, x_ready)
        p.barrier("p1")
        tap_sb("xc", xcv, KH, NT, "ac", p1_done)
        p.wait("vector", "dsl", bias_ready)
        vbqs = p.op("vector",
                    (lambda e: e.tensor_scalar(bq_s[:, :], bq_s[:, :], SCALE,
                                               None, ALU.mult)),
                    inc="ve")
        p.wait("scalar", "ve", vbqs)
        p2a_done = proj(wq, KH, xcv, qv, bq_s, x_ready, rhs_ready_ac=p1_done,
                        evac_scale=SCALE)
        p2b_done = proj(wk, KH, xcv, kv, bk_s, x_ready)
        p.barrier("p2ab")
        tap_sb("q", qv, KH, NT, "ac", p2a_done)
        tap_sb("k", kv, KH, NT, "ac", p2b_done)

        p.wait("tensor", "dg", wv_ready + 32)
        p.wait("tensor", "ve", v_ones)
        for tm in range(KD):
            for n2 in range(2):
                g = tm * 2 + n2
                bank = g % 2
                bank_wait("tensor", bank)
                for k8 in range(KH):
                    p.op("tensor", MM(psb(bank),
                                      xcv(k8)[:, 128 * tm:128 * tm + 128],
                                      wvs(k8, n2), k8 == 0, False))
                vmm = p.op("tensor", MM(psb(bank), ones_rowb[0:1, :],
                                        bv_b[0:1, 512 * n2:512 * n2 + 512],
                                        False, True), inc="pe")
                p2c_last_pe = vmm
                p.wait("scalar", "pe", vmm)
                ev = p.op("scalar",
                          (lambda e, tm_=tm, n2_=n2, bk=bank:
                           e.activation(vv(tm_)[:, 512 * n2_:512 * n2_ + 512],
                                        psb(bk), AF.Identity)),
                          inc="ac")
                bank_war[bank] = ("ac", ev)
                v_done = ev

        sp_war = {0: None, 1: None}
        pt_war = {0: None, 1: None}
        ppt_war = {0: None, 1: None}
        psS_war = {0: None, 1: None}
        psC_war = {0: None, 1: None}
        p.barrier("p2c")
        tap_sb("v", vv, KD, HC, "ac", v_done)
        ctx_done = None

        p.wait("gpsimd", "pe", p2c_last_pe)
        for tqc in range(4):
            p.dma("gpsimd", Bb[:, 512 * tqc:512 * tqc + 512],
                  bmat_in[128 * tqc:128 * tqc + 128, :], "dg")
        bmarg = pad_g()
        p.wait("tensor", "ac", v_done)
        p.wait("tensor", "dg", bmarg)
        mst_war = {}
        lst_war = {}
        vpn_of = {}
        ptevac_of = {}

        SBANK = (0, 1, 6, 7)

        def attn_A(wh):
            w, h = wh // HEADS, wh % HEADS
            c, o = h // 2, 64 * (h % 2)
            sset = wh % 2
            vexp = None
            for tqc in range(4):
                pslot = sset * 4 + tqc
                sidx = sset * 4 + tqc
                bank = SBANK[tqc]
                bank_wait("tensor", bank)
                q_sl = qv(c)[o:o + 64, 512 * w + 128 * tqc:512 * w + 128 * tqc + 128]
                k_sl = kv(c)[o:o + 64, 512 * w:512 * w + 512]
                p.op("tensor", MM(psb(bank), q_sl, k_sl, True, False))
                vmmS = p.op("tensor",
                            (lambda e, bk=bank, tq=tqc:
                             e.matmul(psb(bk), identb[:, :],
                                      Bb[:, 512 * tq:512 * tq + 512],
                                      start=False, stop=True)),
                            inc="pe")
                p.wait("vector", "pe", vmmS)
                if mst_war.get(sidx):
                    p.wait("vector", "ac", mst_war[sidx])
                vmax = p.op("vector",
                            (lambda e, bk=bank, si=sidx:
                             e.reduce_max(mst[:, si:si + 1], psb(bk),
                                          axis=AX.X, negate=True)),
                            inc="ve")
                p.wait("scalar", "ve", vmax)
                if pt_war[pslot]:
                    p.wait("scalar", "pe", pt_war[pslot])
                if lst_war.get(sidx):
                    p.wait("scalar", "ve", lst_war[sidx])
                vexp = p.op("scalar",
                            (lambda e, ptsl=pslot, bk=bank, si=sidx:
                             e.activation(Ptv(ptsl)[:, :], psb(bk), AF.Exp,
                                          bias=mst[:, si:si + 1], scale=1.0,
                                          accum_out=lst[:, si:si + 1])),
                            inc="ac")
                bank_war[bank] = ("ac", vexp)
                mst_war[sidx] = vexp
            p.wait("vector", "ac", vexp)
            s4 = sset * 4
            vrl = p.op("vector",
                       (lambda e, s4_=s4:
                        e.reciprocal(rlst[:, s4_:s4_ + 4], lst[:, s4_:s4_ + 4])),
                       inc="ve")
            for tqc in range(4):
                lst_war[sset * 4 + tqc] = vrl
            p.wait("vector", "ve", vrl)
            for tqc in range(4):
                pslot = sset * 4 + tqc
                sidx = sset * 4 + tqc
                vpn_of[(wh, tqc)] = p.op("vector",
                                         (lambda e, ptsl=pslot, si=sidx:
                                          e.tensor_scalar(Ptv(ptsl)[:, :], Ptv(ptsl)[:, :],
                                                          rlst[:, si:si + 1], None,
                                                          ALU.mult)),
                                         inc="ve")

        def attn_B(wh):
            w, h = wh // HEADS, wh % HEADS
            c, o = h // 2, 64 * (h % 2)
            sset = wh % 2
            for tqc in range(4):
                qpar = tqc % 2
                pslot = sset * 4 + tqc
                p.wait("tensor", "ve", vpn_of.pop((wh, tqc)))
                if ppt_war[qpar]:
                    eng_, val_ = ppt_war[qpar]
                    p.wait("tensor", eng_, val_)
                for tkc in range(4):
                    vT = p.op("tensor",
                              (lambda e, ptsl=pslot, qp=qpar, tk=tkc:
                               e.transpose(psb_bf(2, qp)[:, 128 * tk:128 * tk + 128],
                                           Ptv(ptsl)[:, 128 * tk:128 * tk + 128],
                                           identb[:, :])),
                              inc="pe" if tkc == 3 else None)
                pt_war[pslot] = vT
                ev_eng = "vector" if tqc % 2 == 0 else "scalar"
                p.wait(ev_eng, "pe", vT)
                if ev_eng == "vector":
                    ptevac_of[(wh, tqc)] = ("ve", p.op(
                        "vector",
                        (lambda e, qp=qpar, tq=tqc, ss=sset:
                         e.tensor_copy(
                             PTv(ss).rearrange("p (tk c) -> p tk c", c=512)
                             [:, :, 128 * tq:128 * tq + 128],
                             psb_bf(2, qp).rearrange("p (tk c) -> p tk c", c=128))),
                        inc="ve"))
                else:
                    ptevac_of[(wh, tqc)] = ("ac", p.op(
                        "scalar",
                        (lambda e, qp=qpar, tq=tqc, ss=sset:
                         e.activation(
                             PTv(ss).rearrange("p (tk c) -> p tk c", c=512)
                             [:, :, 128 * tq:128 * tq + 128],
                             psb_bf(2, qp).rearrange("p (tk c) -> p tk c", c=128),
                             AF.Identity)),
                        inc="ac"))
                ppt_war[qpar] = ptevac_of[(wh, tqc)]
            for tq_ in range(4):
                eng_, val_ = ptevac_of.pop((wh, tq_))
                p.wait("tensor", eng_, val_)
            cbank = 3 + sset
            if psC_war[sset]:
                p.wait("tensor", "ac", psC_war[sset])
            for tkc in range(4):
                v_sl = vv(4 * w + tkc)[:, 64 * h:64 * h + 64]
                pt_sl = PTv(sset)[:, 512 * tkc:512 * tkc + 512]
                vctx = p.op("tensor", MM(psb(cbank, rows=64), v_sl, pt_sl,
                                         tkc == 0, tkc == 3),
                            inc="pe" if tkc == 3 else None)
            p.wait("scalar", "pe", vctx)
            done = p.op("scalar",
                        (lambda e, c_=c, o_=o, w_=w, cb=cbank:
                         e.activation(ctxv(c_)[o_:o_ + 64, 512 * w_:512 * w_ + 512],
                                      psb(cb, rows=64), AF.Identity)),
                        inc="ac")
            psC_war[sset] = done
            return done

        pt_war = {i: None for i in range(8)}
        attn_A(0)
        for wh in range(1, NW * HEADS):
            attn_A(wh)
            ctx_done = attn_B(wh - 1)
        ctx_done = attn_B(NW * HEADS - 1)
        p3_end_pe = p.cnt["pe"]
        p.barrier("p3")
        tap_sb("ctx", ctxv, KH, NT, "ac", ctx_done)

        p.wait("sync", "pe", p3_end_pe)
        p.wait("tensor", "ac", ctx_done)
        wdre = wd.rearrange("(kc p) o -> p kc o", p=128)
        r_done_dst = None
        r_store = {}
        wready = {}
        xready = {}

        def p4_issue_w(m):
            sl = wslot[m % 4]
            view = sl[:, 0:KH * 128].rearrange("p (kc o) -> p kc o", kc=KH)
            slot_wait("gpsimd", ("projw", m % 4))
            wready[m] = p.dma("gpsimd", view[:, :, :],
                              wdre[:, :, 128 * m:128 * m + 128], "dg")

        def p4_issue_x(m):
            slot_wait("sync", ("xst", m % 2))
            xready[m] = p.dma("sync", xstv(m % 2)[:, :],
                              xT[128 * m:128 * m + 128, :], "dsl")

        p4_issue_w(0)
        p4_issue_w(1)
        p4_issue_x(0)
        for m in range(KD):
            if m + 2 < KD:
                p4_issue_w(m + 2)
            if m + 1 < KD:
                p4_issue_x(m + 1)
            if m + 1 not in wready:
                wready[m + 1] = pad_g()
            if m + 1 not in xready:
                xready[m + 1] = pad_s()
            p.wait("tensor", "dg", wready[m + 1])
            for n in range(NB):
                g = m * NB + n
                bank = g % 2
                bank_wait("tensor", bank)
                for kc in range(KH):
                    vmm = p.op("tensor",
                               MM(psb(bank), wslot[m % 4][:, 128 * kc:128 * kc + 128],
                                  ctxv(kc)[:, 512 * n:512 * n + 512],
                                  kc == 0, kc == KH - 1),
                               inc="pe" if kc == KH - 1 else None)
                p.wait("scalar", "pe", vmm)
                slot_wait("scalar", ("yst", g % 2))
                vev = p.op("scalar",
                           (lambda e, g_=g, bk=bank, mm_=m:
                            e.activation(ystv(g_ % 2)[:, :], psb(bk), AF.Identity,
                                         bias=bd_s[:, mm_:mm_ + 1])),
                           inc="ac")
                bank_war[bank] = ("ac", vev)
                p.wait("vector", "ac", vev)
                p.wait("vector", "dsl", xready[m + 1])
                slot_wait("vector", ("rst_rd", m % 2))
                vtt = p.op("vector",
                           (lambda e, g_=g, mm_=m, nn=n:
                            e.tensor_add(rstv(mm_ % 2)[:, 512 * nn:512 * nn + 512],
                                         ystv(g_ % 2)[:, :],
                                         xstv(mm_ % 2)[:, 512 * nn:512 * nn + 512])),
                           inc="ve")
                war[("yst", g % 2)] = ("ve", vtt)
            war[("projw", m % 4)] = ("pe", vmm)
            war[("xst", m % 2)] = ("ve", vtt)
            p.wait("sync", "ve", vtt)
            r_done_dst = p.dma("sync", rscr[128 * m:128 * m + 128, :],
                               rstv(m % 2)[:, :], "dst")
            r_store[m] = r_done_dst
            war[("rst_rd", m % 2)] = ("dst", r_done_dst)

        def ln(src, dst, col0, ntok, src_store, sum_bank0, sq_bank0):
            nblk = ntok // 512
            rtb_ld = {}

            def rtb_issue(m):
                slot_wait("gpsimd", ("rtb", m % 2))
                p.wait("gpsimd", "dst", src_store[m])
                rtb_ld[m] = p.dma("gpsimd", rtbv(m % 2)[:, 0:ntok],
                                  src[128 * m:128 * m + 128, col0:col0 + ntok], "dg")

            rtb_issue(0)
            for m in range(KD):
                slot = m % 2
                if m + 1 < KD:
                    rtb_issue(m + 1)
                else:
                    rtb_ld[m + 1] = pad_g()
                p.wait("scalar", "dg", rtb_ld[m + 1])
                slot_wait("scalar", ("sqb", slot))
                vsq = p.op("scalar",
                           (lambda e, sl=slot, nt=ntok:
                            e.activation(sqbv(sl)[:, 0:nt], rtbv(sl)[:, 0:nt], AF.Square)),
                           inc="ac")
                p.wait("tensor", "dg", rtb_ld[m + 1])
                if m == 0:
                    for b in range(nblk):
                        bank_wait("tensor", sum_bank0 + b)
                        bank_wait("tensor", sq_bank0 + b)
                for b in range(nblk):
                    p.op("tensor", MM(psrow(sum_bank0 + b), ones_colb[:, :],
                                      rtbv(slot)[:, 512 * b:512 * b + 512],
                                      m == 0, m == KD - 1))
                p.wait("tensor", "ac", vsq)
                for b in range(nblk):
                    vst_pe = p.op("tensor", MM(psrow(sq_bank0 + b), ones_colb[:, :],
                                               sqbv(slot)[:, 512 * b:512 * b + 512],
                                               m == 0, m == KD - 1),
                                  inc="pe" if b == nblk - 1 else None)
                war[("rtb", slot)] = ("pe", vst_pe)
                war[("sqb", slot)] = ("pe", vst_pe)
            stats_pe = vst_pe
            p.barrier("lnmid")
            blk_done = []
            out_done = None
            vebc = {}
            blkrow_war = None
            p.wait("vector", "pe", stats_pe)
            for b in range(nblk):
                mb_bank = sum_bank0 + b
                rb_bank = sq_bank0 + b
                if blkrow_war:
                    p.wait("vector", "pe", blkrow_war)
                p.op("vector",
                     (lambda e, b_=b: e.tensor_scalar(
                         mean_blk[0:1, :], psrow(sum_bank0 + b_),
                         INVD, None, ALU.mult)))
                p.op("vector",
                     (lambda e, b_=b: e.tensor_scalar(
                         rstd_blk[0:1, :], psrow(sq_bank0 + b_),
                         INVD, None, ALU.mult)))
                p.op("vector",
                     (lambda e: e.tensor_mul(wrow[0:1, :], mean_blk[0:1, :],
                                             mean_blk[0:1, :])))
                p.op("vector",
                     (lambda e: e.tensor_sub(rstd_blk[0:1, :], rstd_blk[0:1, :],
                                             wrow[0:1, :])))
                vvar = p.op("vector",
                            (lambda e: e.tensor_scalar(
                                rstd_blk[0:1, :], rstd_blk[0:1, :],
                                EPS, None, ALU.add)),
                            inc="ve")
                p.wait("scalar", "ve", vvar)
                vstd = p.op("scalar",
                            (lambda e: e.activation(rstd_blk[0:1, :], rstd_blk[0:1, :],
                                                    AF.Sqrt)),
                            inc="ac")
                p.wait("vector", "ac", vstd)
                v_rstd = p.op("vector",
                              (lambda e: e.reciprocal(rstd_blk[0:1, :],
                                                      rstd_blk[0:1, :])),
                              inc="ve")
                bank_war[sum_bank0 + b] = ("ve", v_rstd)
                bank_war[sq_bank0 + b] = ("ve", v_rstd)
                p.wait("tensor", "ve", v_rstd)
                bank_wait("tensor", mb_bank)
                bank_wait("tensor", rb_bank)
                p.op("tensor", MM(psb(mb_bank), ones_row32[0:1, :],
                                  mean_blk[0:1, :], True, True))
                vbc = p.op("tensor", MM(psb(rb_bank), ones_row32[0:1, :],
                                        rstd_blk[0:1, :], True, True),
                           inc="pe")
                blkrow_war = vbc
                p.wait("vector", "pe", vbc)
                p.op("vector", (lambda e, b_=b, mb=mb_bank:
                                e.tensor_copy(mbc_s(b_)[:, :], psb(mb))))
                vebc[b] = p.op("vector", (lambda e, b_=b, rb=rb_bank:
                                          e.tensor_copy(rbc_s(b_)[:, :], psb(rb))),
                               inc="ve")
                bank_war[mb_bank] = ("ve", vebc[b])
                bank_war[rb_bank] = ("ve", vebc[b])
            for b in range(nblk):
                nlds = {}

                def nm_issue(m):
                    slot_wait("sync", ("xst", m % 2))
                    p.wait("sync", "dst", src_store[m])
                    nlds[m] = p.dma("sync", xstv(m % 2)[:, 0:512],
                                    src[128 * m:128 * m + 128,
                                        col0 + 512 * b:col0 + 512 * b + 512], "dsl")

                nm_issue(0)
                p.wait("vector", "ve", vebc[b])
                for m in range(KD):
                    slot = m % 2
                    if m + 1 < KD:
                        nm_issue(m + 1)
                    else:
                        nlds[m + 1] = pad_s()
                    p.wait("vector", "dsl", nlds[m + 1])
                    slot_wait("vector", ("rst_rd", slot))
                    p.op("vector",
                         (lambda e, sl=slot, b_=b:
                          e.tensor_sub(rstv(sl)[:, 0:512], xstv(sl)[:, 0:512],
                                       mbc_s(b_)[:, :])))
                    vn = p.op("vector",
                              (lambda e, sl=slot, b_=b:
                               e.tensor_mul(rstv(sl)[:, 0:512], rstv(sl)[:, 0:512],
                                            rbc_s(b_)[:, :])),
                              inc="ve")
                    war[("xst", slot)] = ("ve", vn)
                    p.wait("sync", "ve", vn)
                    out_done = p.dma("sync", dst[128 * m:128 * m + 128,
                                              col0 + 512 * b:col0 + 512 * b + 512],
                                     rstv(slot)[:, 0:512], "dst")
                    war[("rst_rd", slot)] = ("dst", out_done)
                blk_done.append(out_done)
            return blk_done

        p.barrier("p4")
        tap_dram("r", rscr, r_done_dst)
        h1blk = ln(rscr, h1scr, 0, NT, r_store, 0, 4)
        h1_ready = h1blk[-1]
        p.barrier("ln1")
        tap_dram("h1", h1scr, h1_ready)

        h1b_war = None
        r2_store = {}
        pend_ln2 = None

        def emit_gate(pp):
            nonlocal h1b_war
            cols = 512 * pp
            p.wait("gpsimd", "dst", h1blk[pp])
            if h1b_war:
                p.wait("gpsimd", "pe", h1b_war)
            for m in range(KD):
                h1b_ready = p.dma("gpsimd", h1bv(m)[:, :],
                                  h1scr[128 * m:128 * m + 128, cols:cols + 512], "dg")
            wgure = wgu.rearrange("(kc p) o -> p kc o", p=128)
            guready = {}

            def gate_issue(j):
                gview = wslot[j % 4][:, :].rearrange("p (kc o) -> p kc o", kc=KD)
                uview = uslot[j % 4][:, :].rearrange("p (kc o) -> p kc o", kc=KD)
                slot_wait("gpsimd", ("projw", j % 4))
                p.dma("gpsimd", gview[:, 0:8, :], wgure[:, 0:8, 128 * j:128 * j + 128], "dg")
                p.dma("gpsimd", gview[:, 8:16, :], wgure[:, 8:16, 128 * j:128 * j + 128], "dg")
                slot_wait("gpsimd", ("ubuf", j % 4))
                p.dma("gpsimd", uview[:, 0:8, :],
                      wgure[:, 0:8, INTER + 128 * j:INTER + 128 * j + 128], "dg")
                guready[j] = p.dma("gpsimd", uview[:, 8:16, :],
                                   wgure[:, 8:16, INTER + 128 * j:INTER + 128 * j + 128],
                                   "dg")

            gate_issue(0)
            gate_issue(1)
            for j in range(CI):
                gsl = wslot[j % 4]
                usl = uslot[j % 4]
                if j + 2 < CI:
                    gate_issue(j + 2)
                if j + 1 not in guready:
                    guready[j + 1] = pad_g()
                gbank = j % 2
                ubank = 2 + j % 2
                p.wait("tensor", "dg", max(guready[j + 1], h1b_ready))
                bank_wait("tensor", gbank)
                for kc in range(KD):
                    vg = p.op("tensor", MM(psb(gbank), gsl[:, 128 * kc:128 * kc + 128],
                                           h1bv(kc)[:, :], kc == 0, kc == KD - 1),
                              inc="pe" if kc == KD - 1 else None)
                bank_wait("tensor", ubank)
                for kc in range(KD):
                    vu = p.op("tensor", MM(psb(ubank), usl[:, 128 * kc:128 * kc + 128],
                                           h1bv(kc)[:, :], kc == 0, kc == KD - 1),
                              inc="pe" if kc == KD - 1 else None)
                war[("projw", j % 4)] = ("pe", vg)
                war[("ubuf", j % 4)] = ("pe", vu)
                p.wait("scalar", "pe", vg)
                slot_wait("scalar", ("silu", j % 2))
                vsi = p.op("scalar",
                           (lambda e, j_=j, bk=gbank:
                            e.activation(siluv(j_ % 2)[:, :], psb(bk), AF.Silu)),
                           inc="ac")
                bank_war[gbank] = ("ac", vsi)
                p.wait("vector", "ac", vsi)
                p.wait("vector", "pe", vu)
                vact = p.op("vector",
                            (lambda e, j_=j, bk=ubank:
                             e.tensor_mul(actv(j_)[:, :], siluv(j_ % 2)[:, :], psb(bk))),
                            inc="ve")
                war[("silu", j % 2)] = ("ve", vact)
                bank_war[ubank] = ("ve", vact)
                act_ready[j] = vact
            h1b_war = vu
            p.barrier("gate")

        def emit_down(pp):
            cols = 512 * pp
            r2s = r2_store.setdefault(pp, {})
            for mg in range(2):
                dnready = {}

                def dn_issue(j, mg_=mg):
                    slot_wait("gpsimd", ("dnbuf", j % 4))
                    dnready[j] = p.dma("gpsimd", dnslot[j % 4][:, :],
                                       wdn[128 * j:128 * j + 128,
                                           1024 * mg_:1024 * mg_ + 1024], "dg")

                dn_issue(0)
                dn_issue(1)
                for j in range(CI):
                    dsl_slot = dnslot[j % 4]
                    if j + 2 < CI:
                        dn_issue(j + 2)
                    if j + 1 not in dnready:
                        dnready[j + 1] = pad_g()
                    p.wait("tensor", "dg", dnready[j + 1])
                    p.wait("tensor", "ve", act_ready[j])
                    if j == 0:
                        for mm in range(8):
                            bank_wait("tensor", mm)
                    for mm in range(8):
                        vdn = p.op("tensor",
                                   MM(psb(mm), dsl_slot[:, 128 * mm:128 * mm + 128],
                                      actv(j)[:, :], j == 0, j == CI - 1),
                                   inc="pe" if mm == 7 else None)
                    war[("dnbuf", j % 4)] = ("pe", vdn)
                h1lds = {}

                def h1_issue(mm, mg_=mg):
                    m_ = 8 * mg_ + mm
                    slot_wait("sync", ("xst", m_ % 2))
                    p.wait("sync", "dst", h1blk[pp])
                    h1lds[mm] = p.dma("sync", xstv(m_ % 2)[:, 0:512],
                                      h1scr[128 * m_:128 * m_ + 128, cols:cols + 512],
                                      "dsl")

                h1_issue(0)
                for mm in range(8):
                    m = 8 * mg + mm
                    slot = m % 2
                    if mm + 1 < 8:
                        h1_issue(mm + 1)
                    else:
                        h1lds[mm + 1] = pad_s()
                    p.wait("scalar", "pe", vdn)
                    slot_wait("scalar", ("yst", mm % 2))
                    vev = p.op("scalar",
                               (lambda e, mm_=mm: e.activation(
                                   ystv(mm_ % 2)[:, :], psb(mm_), AF.Identity)),
                               inc="ac")
                    bank_war[mm] = ("ac", vev)
                    p.wait("vector", "ac", vev)
                    p.wait("vector", "dsl", h1lds[mm + 1])
                    slot_wait("vector", ("rst_rd", slot))
                    vr2 = p.op("vector",
                               (lambda e, mm_=mm, sl=slot: e.tensor_add(
                                   rstv(sl)[:, 0:512], ystv(mm_ % 2)[:, :],
                                   xstv(sl)[:, 0:512])),
                               inc="ve")
                    war[("xst", slot)] = ("ve", vr2)
                    war[("yst", mm % 2)] = ("ve", vr2)
                    p.wait("sync", "ve", vr2)
                    r2s[m] = p.dma("sync", r2scr[128 * m:128 * m + 128, cols:cols + 512],
                                   rstv(slot)[:, 0:512], "dst")
                    war[("rst_rd", slot)] = ("dst", r2s[m])
            p.barrier("down")
            if pp == 0 and "act" in taps:
                p.wait("gpsimd", "ve", p.cnt["ve"])
                for j in range(CI):
                    p.dma("gpsimd", taps["act"][128 * j:128 * j + 128, 0:512],
                          actv(j)[:, 0:512], "dg")

        act_ready = {}
        for pp in range(NB):
            if pend_ln2 is not None:
                ln(r2scr, outT, 512 * pend_ln2, 512, r2_store[pend_ln2], 6, 7)
                p.barrier("ln2")
            emit_gate(pp)
            emit_down(pp)
            pend_ln2 = pp
        ln(r2scr, outT, 512 * pend_ln2, 512, r2_store[pend_ln2], 6, 7)
        tap_dram("r2", r2scr, r2_store[NB - 1][KD - 1])

        p.op("sync", lambda e: e.nop())
        p.wait("sync", "dst", p.cnt["dst"])
        p.wait("sync", "dg", p.cnt["dg"])

        with nc.Block() as block:

            @block.sync
            def _(e):
                p.replay("sync", e, semh)

            @block.gpsimd
            def _(e):
                p.replay("gpsimd", e, semh)

            @block.tensor
            def _(e):
                p.replay("tensor", e, semh)

            @block.vector
            def _(e):
                p.replay("vector", e, semh)

            @block.scalar
            def _(e):
                p.replay("scalar", e, semh)

    return nc


_NC_CACHE = None


def _get_nc():
    global _NC_CACHE
    if _NC_CACHE is None:
        _NC_CACHE = build()
    return _NC_CACHE


def kernel(**inputs):
    x = np.asarray(inputs["L0_attn_hidden_states"], np.float32).reshape(NCORE * NT, D)
    wc = np.ascontiguousarray(np.asarray(inputs["L0_attn_Wc"], np.float32))
    wq = np.ascontiguousarray(np.asarray(inputs["L0_attn_Wq"], np.float32))
    wk = np.ascontiguousarray(np.asarray(inputs["L0_attn_Wk"], np.float32))
    wv = np.ascontiguousarray(np.asarray(inputs["L0_attn_Wv"], np.float32))
    wd = np.ascontiguousarray(np.asarray(inputs["L0_attn_Wd"], np.float32))
    wgu = np.ascontiguousarray(np.asarray(inputs["L0_ffn_w_gate_up"], np.float32).T)
    wdn = np.ascontiguousarray(np.asarray(inputs["L0_ffn_w_down"], np.float32).T)
    bc = np.ascontiguousarray(np.asarray(inputs["L0_attn_bc"], np.float32).reshape(KH, 128).T)
    bq = np.ascontiguousarray(np.asarray(inputs["L0_attn_bq"], np.float32).reshape(KH, 128).T)
    bk = np.ascontiguousarray(np.asarray(inputs["L0_attn_bk"], np.float32).reshape(KH, 128).T)
    bd = np.ascontiguousarray(np.asarray(inputs["L0_attn_bd"], np.float32).reshape(KD, 128).T)
    bvr = np.ascontiguousarray(np.asarray(inputs["L0_attn_bv"], np.float32).reshape(1, HC))
    rb = np.asarray(inputs["L0_attn_rel_bias"], np.float32)
    Ev = np.concatenate([np.full(T - 129, rb[0], np.float32), rb,
                         np.full(T - 129, rb[-1], np.float32)])
    S = np.lib.stride_tricks.sliding_window_view(Ev, T)
    bmat = np.ascontiguousarray(S[::-1]).astype(np.float32)
    ident = np.eye(128, dtype=np.float32)

    nc = _get_nc()
    shared = dict(wc=wc, wq=wq, wk=wk, wv=wv, wd=wd, wgu=wgu, wdn=wdn,
                  bc=bc, bq=bq, bk=bk, bd=bd, bvr=bvr, bmat=bmat, ident=ident)
    in_maps = []
    for c in range(NCORE):
        xT = np.ascontiguousarray(x[NT * c:NT * (c + 1)].T)
        in_maps.append(dict(xT=xT, **shared))
    res = run_bass_kernel_spmd(nc, in_maps, core_ids=list(range(NCORE)))
    out = np.empty((NCORE * NT, D), np.float32)
    for c in range(NCORE):
        out[NT * c:NT * (c + 1)] = np.asarray(res.results[c]["outT"]).T
    return out.reshape(4, 4096, D)
